# version 28
# baseline (speedup 1.0000x reference)
import numpy as np
from contextlib import ExitStack

import concourse.bass as bass
import concourse.mybir as mybir
from concourse.bass_utils import run_bass_kernel_spmd

F32 = mybir.dt.float32
BF16 = mybir.dt.bfloat16
AX = mybir.AxisListType.X
OP = mybir.AluOpType
AF = mybir.ActivationFunctionType

D = 4096
NC = 32
TOK = 2048
NT = 16
NG = 8
NE = 8
NH = 128
EPS = 1e-5
OCW = 512
NOC = D // OCW
WC = 144
WC2 = 2 * WC
COL_ONES = 136

STATS = ("s2", "m1n", "v1", "msq", "v1b", "ln1", "inv1", "a1", "b1",
         "tt2", "v2", "ln2", "inv2", "alpha", "alph")
GST = ("s2r", "v1", "msq", "v1b", "a1", "b1", "tt2", "v2", "alpha", "alph",
       "lga", "lgb", "rneg",
       "top8a", "top8b", "mask1", "mask2",
       "tsum", "tinv", "g1", "g1b", "g2", "g2b", "w8a", "w8b", "w8f")
VPRE = ("eps", "padz", "ones136", "ones128", "whl_h", "whl_l", "wsr", "bh", "bl")


class Tok:
    def __init__(self):
        self.n = {"T": 0, "S": 0, "V": 0, "G": 0}
        self.named = {}

    def inc(self, eng, name=None):
        self.n[eng] += 1
        if name is not None:
            key = (eng,) + name
            assert key not in self.named, key
            self.named[key] = self.n[eng]
        return self.n[eng]

    def get(self, eng, *name):
        return self.named[(eng,) + name]


def plan_tokens():
    tk = Tok()
    tk.inc("T", ("pre", "wsums"))
    for nm in VPRE:
        tk.inc("V", ("pre", nm))
    def t_TR(i):
        for q in range(8):
            tk.inc("T", ("tr", i, q))
    def t_Y(i):
        for oc in range(NOC):
            tk.inc("T", ("y", i, oc))
            if i + 4 < NT:
                tk.inc("T", ("tr", i + 4, oc))
    t_TR(0); t_TR(1)
    for g in range(NG + 1):
        if 2 * g < NT:
            tk.inc("T", ("xa", 2 * g)); tk.inc("T", ("xa", 2 * g + 1))
        if g == 0:
            t_TR(2); t_TR(3)
        if g >= 1:
            a, b = 2 * g - 2, 2 * g - 1
            tk.inc("T", ("hwt", a)); t_Y(a)
            tk.inc("T", ("hwt", b)); t_Y(b)
    def s_ev_ids(i, q0=0, q1=8):
        for q in range(q0, q1):
            tk.inc("S", ("ev", i, q))
    def s_sx2(i):
        tk.inc("S", ("sx2", i, 0))
    for g in range(NG + 1):
        if g >= 1:
            for j in (2 * g - 2, 2 * g - 1):
                tk.inc("S", ("tmp", j, 0))
                tk.inc("S", ("tmp", j, 1))
        if g == 0:
            s_ev_ids(0); s_sx2(0); s_ev_ids(1); s_sx2(1)
            s_ev_ids(2); s_sx2(2); s_ev_ids(3)
        if 1 <= g < NG - 1:
            s_ev_ids(2 * g + 2)
            s_ev_ids(2 * g + 3)
            s_sx2(2 * g + 2)
            s_sx2(2 * g + 3)
        if g < NG:
            tk.inc("S", ("ln1", g))
            tk.inc("S", ("inv1", g))
            tk.inc("S", ("ln2", g))
            tk.inc("S", ("inv2", g))
        if g < NG:
            tk.inc("S", ("exp", g, 0))
            tk.inc("S", ("exp", g, 1))
        if g == 0:
            s_sx2(3)
    def v_hw(j):
        for _ in range(NE - 1):
            tk.inc("V")
        tk.inc("V", ("hw", j))
    def v_split(j):
        tk.inc("V", ("hwth", j))
        tk.inc("V", ("hwtl", j))
    def v_tail2(j, o0, o1):
        for oc in range(o0, o1):
            tk.inc("V", ("epi", j, oc))
            if j + 4 < NT:
                tk.inc("V", ("lo", j + 4, oc))
    ST1 = ("v1", "msq", "v1b")
    ST2 = ("a1", "b1", "tt2", "v2")
    ST3 = ("alpha", "alph", "lga", "lgb", "rneg")
    ST4 = ("top8a", "top8b", "mask1", "mask2", "tsum", "tinv",
           "g1", "g1b", "g2", "g2b", "w8a", "w8b", "w8f")
    for i in range(4):
        for q in range(8):
            tk.inc("V", ("lo", i, q))
    def v_corr_p(i):
        tk.inc("V", ("m1n", i))
        tk.inc("V", ("corr", i))
        tk.inc("V", ("corr2", i))
    for g in range(NG + 1):
        if 2 * g < NT:
            v_corr_p(2 * g)
        if g >= 1:
            v_hw(2 * g - 2)
            v_hw(2 * g - 1)
            v_split(2 * g - 2)
        if 2 * g < NT:
            v_corr_p(2 * g + 1)
        if g >= 1:
            v_tail2(2 * g - 2, 0, 7)
            v_split(2 * g - 1)
            v_tail2(2 * g - 2, 7, NOC)
            v_tail2(2 * g - 1, 0, NOC)
        if g < NG:
            for nm in ST1 + ST2 + ST3 + ST4:
                tk.inc("V", ("st", g, nm))
    return tk


def build_nc():
    tk = plan_tokens()
    nc = bass.Bass("TRN2", target_bir_lowering=False, debug=False)

    xs = nc.dram_tensor("xs", [TOK, D], F32, kind="ExternalInput")
    a_fm = nc.dram_tensor("a_fm", [D, NH], F32, kind="ExternalInput")
    rw_fm = nc.dram_tensor("rw_fm", [D, NE], F32, kind="ExternalInput")
    b_fm = nc.dram_tensor("b_fm", [NH, D], F32, kind="ExternalInput")
    rb_rep = nc.dram_tensor("rb_rep", [128, NE], F32, kind="ExternalInput")
    iden = nc.dram_tensor("iden", [128, 128], F32, kind="ExternalInput")
    out = nc.dram_tensor("out", [TOK, D], F32, kind="ExternalOutput")

    es = ExitStack()
    def sb(name, shape, dt=F32):
        return es.enter_context(nc.sbuf_tensor(name, shape, dt))
    def ps(name, shape, dt=F32):
        return es.enter_context(nc.psum_tensor(name, shape, dt))
    def sem(name):
        return es.enter_context(nc.semaphore(name))

    sb_x = [sb(f"sb_x{p}", [128, 2 * D]) for p in range(3)]
    sb_xh = [sb(f"sb_xh{p}", [128, D], BF16) for p in range(2)]
    sb_xl = [sb(f"sb_xl{p}", [128, D], BF16) for p in range(2)]
    sb_wf = sb("sb_wf", [128, NC * WC])
    sb_whl = sb("sb_whl", [128, NC * WC2], BF16)
    sb_bf32 = sb("sb_bf32", [128, D])
    sb_bh = sb("sb_bh", [128, D], BF16)
    sb_bl = sb("sb_bl", [128, D], BF16)
    sb_wsr = sb("sb_wsr", [128, WC])
    sb_rb = sb("sb_rb", [128, NE])
    sb_id = sb("sb_id", [128, 128])
    sb_ones = sb("sb_ones", [128, 128])
    sb_eps = sb("sb_eps", [128, 1])
    sb_xap = sb("sb_xap", [128, 4 * WC])
    sb_hw = [sb(f"sb_hw{p}", [128, 128]) for p in range(2)]
    sb_hwth = [sb(f"sb_hwth{p}", [128, 128], BF16) for p in range(2)]
    sb_hwtl = [sb(f"sb_hwtl{p}", [128, 128], BF16) for p in range(2)]
    stats = {nm: sb("st_" + nm, [128, 4 if nm in ('a1', 'b1') else 2]) for nm in STATS}
    sb_s2p = sb("sb_s2p", [128, 16])
    gat = {nm: sb("gt_" + nm, [128, 4 * NE if nm == "w8f" else 2 * NE]) for nm in
           ("lg", "lg2", "expd", "top8", "mask1", "mask2", "w8a", "w8b", "w8f")}
    gsc = {nm: sb("gs_" + nm, [128, 2]) for nm in
           ("rneg", "tsum", "tinv", "g1", "g1b", "g2", "g2b")}
    sb_tm = [sb_wf, sb_bf32]

    ps_xt = [ps(f"ps_xt{p}", [128, 512]) for p in range(2)]
    ps_xa = [ps(f"ps_xa{p}", [128, 512]) for p in range(2)]
    ps_y = [ps(f"ps_y{p}", [128, OCW]) for p in range(4)]
    ps_hw = [ps_xa[0][:, 384:512], ps_xa[1][:, 384:512]]

    sd_w = sem("sd_w")
    sd_in = [sem(f"sd_in{g}") for g in range(NG)]
    sd_out = [sem(f"sd_out{g}") for g in range(NG)]
    st = sem("st"); sv = sem("sv"); ss = sem("ss")

    def ap3(t_, off, pstep, d1, n1, d2, n2):
        return bass.AP(t_, off, [[pstep, 128], [d1, n1], [d2, n2]])
    def g16(t_):
        return ap3(t_, 0, 16, 8, 2, 1, 8)
    def g2x(t_):
        return ap3(t_, 0, 2, 1, 2, 1, 1)

    with es:
        with nc.Block() as block:

            @block.sync
            def _(sync):
                sync.dma_start(
                    bass.AP(sb_wf, 0, [[NC * WC, 128], [WC, NC], [1, NH]]),
                    bass.AP(a_fm, 0, [[NH, 128], [128 * NH, NC], [1, NH]]),
                ).then_inc(sd_w, 16)
                sync.dma_start(
                    bass.AP(sb_wf, 128, [[NC * WC, 128], [WC, NC], [1, NE]]),
                    bass.AP(rw_fm, 0, [[NE, 128], [128 * NE, NC], [1, NE]]),
                ).then_inc(sd_w, 16)
                sync.dma_start(sb_bf32[:, :], b_fm[:, :]).then_inc(sd_w, 16)
                sync.dma_start(sb_rb[:, :], rb_rep[:, :]).then_inc(sd_w, 16)
                sync.dma_start(sb_id[:, :], iden[:, :]).then_inc(sd_w, 16)
                for g in range(3):
                    for il in range(2):
                        sync.dma_start(
                            bass.AP(sb_x[g % 3], il * D, [[2 * D, 128], [1, D]]),
                            bass.AP(xs, (g * 2 + il) * 128 * D, [[D, 128], [1, D]]),
                        ).then_inc(sd_in[g], 16)
                HD = D // 2
                for g in range(NG):
                    for il in range(2):
                        i = g * 2 + il
                        for h in range(2):
                            sync.wait_ge(sv, tk.get("V", "epi", i, 4 * h + 3))
                            sync.dma_start(
                                bass.AP(out, i * 128 * D + h * HD, [[D, 128], [1, HD]]),
                                bass.AP(sb_x[g % 3], il * D + h * HD,
                                        [[2 * D, 128], [1, HD]]),
                            ).then_inc(sd_out[g], 16)
                        if g + 3 < NG:
                            sync.wait_ge(sd_out[g], 32 * (il + 1))
                            sync.dma_start(
                                bass.AP(sb_x[g % 3], il * D, [[2 * D, 128], [1, D]]),
                                bass.AP(xs, ((g + 3) * 2 + il) * 128 * D, [[D, 128], [1, D]]),
                            ).then_inc(sd_in[g + 3], 16)
                for g in range(NG):
                    sync.wait_ge(sd_out[g], 64)

            @block.vector
            def _(v):
                n = [0]
                pend = [False]
                def vi(instr, name=None):
                    if pend[0]:
                        instr._wait_ge(sv, n[0])
                        pend[0] = False
                    instr.then_inc(sv, 1)
                    n[0] += 1
                    if name is not None:
                        assert tk.get("V", *name) == n[0], (name, n[0], tk.get("V", *name))
                def vc():
                    pend[0] = True

                vi(v.memset(sb_eps[:, :], EPS), ("pre", "eps"))
                v.wait_ge(sd_w, 5 * 16)
                vi(v.memset(bass.AP(sb_wf, 137, [[NC * WC, 128], [WC, NC], [1, 7]]), 0.0),
                   ("pre", "padz"))
                vc()
                vi(v.memset(bass.AP(sb_wf, COL_ONES, [[NC * WC, 128], [WC, NC], [1, 1]]), 1.0),
                   ("pre", "ones136"))
                vi(v.memset(sb_ones[:, :], 1.0), ("pre", "ones128"))
                vc()
                vi(v.tensor_copy(
                    bass.AP(sb_whl, 0, [[NC * WC2, 128], [WC2, NC], [1, WC]]),
                    bass.AP(sb_wf, 0, [[NC * WC, 128], [WC, NC], [1, WC]])),
                   ("pre", "whl_h"))
                vc()
                vi(v.tensor_tensor(
                    bass.AP(sb_whl, WC, [[NC * WC2, 128], [WC2, NC], [1, WC]]),
                    bass.AP(sb_wf, 0, [[NC * WC, 128], [WC, NC], [1, WC]]),
                    bass.AP(sb_whl, 0, [[NC * WC2, 128], [WC2, NC], [1, WC]]),
                    OP.subtract), ("pre", "whl_l"))
                v.wait_ge(st, tk.get("T", "pre", "wsums"))
                vi(v.tensor_copy(sb_wsr[:, :], ps_xa[0][:, 0:WC]), ("pre", "wsr"))
                vi(v.tensor_copy(sb_bh[:, :], sb_bf32[:, :]), ("pre", "bh"))
                vc()
                vi(v.tensor_tensor(sb_bl[:, :], sb_bf32[:, :], sb_bh[:, :], OP.subtract),
                   ("pre", "bl"))

                def v_lo(i, q):
                    p = i % 2
                    vi(v.tensor_tensor(sb_xl[p][:, q * 512:(q + 1) * 512],
                                       ps_xt[q % 2][:, :],
                                       sb_xh[p][:, q * 512:(q + 1) * 512],
                                       OP.subtract)
                       ._wait_ge(ss, tk.get("S", "ev", i, q)), ("lo", i, q))

                for i in range(4):
                    for q in range(8):
                        v_lo(i, q)

                def v_corr(i):
                    g, il, sl = i // 2, i % 2, i % 4
                    v.wait_ge(st, tk.get("T", "xa", i))
                    vi(v.tensor_scalar(stats["m1n"][:, il:il + 1],
                                       ps_xa[i % 2][:, COL_ONES:COL_ONES + 1],
                                       -1.0 / D, None, OP.mult), ("m1n", i))
                    vc()
                    vi(v.scalar_tensor_tensor(sb_xap[:, sl * WC:(sl + 1) * WC],
                                              sb_wsr[:, :], stats["m1n"][:, il:il + 1],
                                              ps_xa[i % 2][:, 0:WC], OP.mult, OP.add),
                       ("corr", i))
                    vc()
                    vi(v.tensor_tensor(sb_xap[:, sl * WC:(sl + 1) * WC],
                                       sb_xap[:, sl * WC:(sl + 1) * WC],
                                       ps_xa[i % 2][:, WC:2 * WC], OP.add), ("corr2", i))

                def v_hw(j):
                    gj, jl, sl = j // 2, j % 2, j % 4
                    wsl = (gj % 2) * 16 + jl * 8
                    pj = j % 2
                    vc()
                    if j >= 2:
                        v.wait_ge(st, tk.get("T", "hwt", j - 2))
                    for e in range(NE):
                        ins = v.tensor_scalar(
                            sb_hw[pj][:, e * 16:(e + 1) * 16],
                            sb_xap[:, sl * WC + e * 16:sl * WC + (e + 1) * 16],
                            gat["w8f"][:, wsl + e:wsl + e + 1],
                            None, OP.mult)
                        if e == NE - 1:
                            vi(ins, ("hw", j))
                        elif e == 0:
                            vi(ins)
                        else:
                            ins.then_inc(sv, 1); n[0] += 1

                def v_split(j):
                    pj = j % 2
                    v.wait_ge(st, tk.get("T", "hwt", j))
                    vi(v.tensor_copy(sb_hwth[pj][:, :], ps_hw[pj]), ("hwth", j))
                    vc()
                    vi(v.tensor_tensor(sb_hwtl[pj][:, :], ps_hw[pj],
                                       sb_hwth[pj][:, :], OP.subtract), ("hwtl", j))

                def v_tail2(j, o0, o1):
                    gj, jl = j // 2, j % 2
                    pj = j % 2
                    if o0 == 0:
                        v.wait_ge(ss, tk.get("S", "tmp", j, 0))
                    for oc in range(o0, o1):
                        if oc == 4:
                            v.wait_ge(ss, tk.get("S", "tmp", j, 1))
                        vi(v.tensor_add(
                            sb_x[gj % 3][:, jl * D + oc * OCW:jl * D + (oc + 1) * OCW],
                            sb_tm[pj][:, oc * OCW:(oc + 1) * OCW],
                            ps_y[oc % 4][:, :])
                           ._wait_ge(st, tk.get("T", "y", j, oc)), ("epi", j, oc))
                        if j + 4 < NT:
                            v_lo(j + 4, oc)

                for g in range(NG + 1):
                    if 2 * g < NT:
                        v_corr(2 * g)
                    if g >= 1:
                        v_hw(2 * g - 2)
                        v_hw(2 * g - 1)
                        v_split(2 * g - 2)
                    if 2 * g < NT:
                        v_corr(2 * g + 1)
                    if g >= 1:
                        v_tail2(2 * g - 2, 0, 7)
                        v_split(2 * g - 1)
                        v_tail2(2 * g - 2, 7, NOC)
                        v_tail2(2 * g - 1, 0, NOC)
                    ab = 2 * (g % 2)
                    i = 2 * g + 1
                    if g < NG:
                        v.wait_ge(ss, tk.get("S", "sx2", i, 0))
                        vi(v.tensor_scalar(stats["v1"][:, :],
                                           sb_s2p[:, (g % 2) * 2:(g % 2) * 2 + 2],
                                           1.0 / D, None, OP.mult), ("st", g, "v1"))
                        vc(); vi(v.tensor_mul(stats["msq"][:, :], stats["m1n"][:, :],
                                              stats["m1n"][:, :]), ("st", g, "msq"))
                        vc(); vi(v.tensor_sub(stats["v1b"][:, :], stats["v1"][:, :],
                                              stats["msq"][:, :]), ("st", g, "v1b"))
                    if g < NG:
                        v.wait_ge(ss, tk.get("S", "inv1", g))
                        vi(v.tensor_scalar(stats["a1"][:, ab:ab + 2], stats["inv1"][:, :],
                                           1.0, None, OP.add), ("st", g, "a1"))
                        vc(); vi(v.tensor_mul(stats["b1"][:, ab:ab + 2], stats["m1n"][:, :],
                                              stats["inv1"][:, :]), ("st", g, "b1"))
                        vc(); vi(v.tensor_mul(stats["tt2"][:, :], stats["a1"][:, ab:ab + 2],
                                              stats["a1"][:, ab:ab + 2]), ("st", g, "tt2"))
                        vc(); vi(v.tensor_mul(stats["v2"][:, :], stats["tt2"][:, :],
                                              stats["v1b"][:, :]), ("st", g, "v2"))
                    if g < NG:
                        v.wait_ge(ss, tk.get("S", "inv2", g))
                        vi(v.tensor_mul(stats["alpha"][:, :], stats["inv2"][:, :],
                                        stats["a1"][:, ab:ab + 2]), ("st", g, "alpha"))
                        vc(); vi(v.tensor_scalar(stats["alph"][:, :], stats["alpha"][:, :],
                                                 0.5, None, OP.mult), ("st", g, "alph"))
                        vc()
                        vi(v.tensor_tensor(g16(gat["lg"]),
                                           ap3(sb_xap, 128, 4 * WC, WC, 2, 1, 8)
                                           if g % 2 == 0 else
                                           ap3(sb_xap, 2 * WC + 128, 4 * WC, WC, 2, 1, 8),
                                           ap3(stats["alpha"], 0, 2, 1, 2, 0, 8),
                                           OP.mult), ("st", g, "lga"))
                        vc()
                        vi(v.tensor_tensor(g16(gat["lg2"]), g16(gat["lg"]),
                                           ap3(sb_rb, 0, 8, 0, 2, 1, 8), OP.add),
                           ("st", g, "lgb"))
                        vc()
                        vi(v.reduce_max(g2x(gsc["rneg"]), g16(gat["lg2"]), AX, negate=True),
                           ("st", g, "rneg"))
                        v.wait_ge(ss, tk.get("S", "exp", g, 1))
                        vi(v.max(gat["top8"][:, 0:8], gat["expd"][:, 0:8]),
                           ("st", g, "top8a"))
                        vc()
                        vi(v.max(gat["top8"][:, 8:16], gat["expd"][:, 8:16]),
                           ("st", g, "top8b"))
                        m1v = bass.AP(gat["top8"], 0, [[16, 128], [8, 2]])
                        m2v = bass.AP(gat["top8"], 1, [[16, 128], [8, 2]])
                        vc()
                        vi(v.tensor_tensor(g16(gat["mask1"]), g16(gat["expd"]),
                                           bass.AP(gat["top8"], 0, [[16, 128], [8, 2], [0, 8]]),
                                           OP.is_equal), ("st", g, "mask1"))
                        vc()
                        vi(v.tensor_tensor(g16(gat["mask2"]), g16(gat["expd"]),
                                           bass.AP(gat["top8"], 1, [[16, 128], [8, 2], [0, 8]]),
                                           OP.is_equal), ("st", g, "mask2"))
                        vc(); vi(v.tensor_tensor(gsc["tsum"][:, :], m1v, m2v, OP.add),
                                 ("st", g, "tsum"))
                        vc(); vi(v.reciprocal(gsc["tinv"][:, :], gsc["tsum"][:, :]),
                                 ("st", g, "tinv"))
                        vc(); vi(v.tensor_tensor(gsc["g1"][:, :], m1v, gsc["tinv"][:, :],
                                                 OP.mult), ("st", g, "g1"))
                        vc(); vi(v.tensor_mul(gsc["g1b"][:, :], gsc["g1"][:, :],
                                              stats["alph"][:, :]), ("st", g, "g1b"))
                        vc(); vi(v.tensor_tensor(gsc["g2"][:, :], m2v, gsc["tinv"][:, :],
                                                 OP.mult), ("st", g, "g2"))
                        vc(); vi(v.tensor_mul(gsc["g2b"][:, :], gsc["g2"][:, :],
                                              stats["alph"][:, :]), ("st", g, "g2b"))
                        vc()
                        vi(v.tensor_tensor(g16(gat["w8a"]), g16(gat["mask1"]),
                                           ap3(gsc["g1b"], 0, 2, 1, 2, 0, 8),
                                           OP.mult), ("st", g, "w8a"))
                        vc()
                        vi(v.tensor_tensor(g16(gat["w8b"]), g16(gat["mask2"]),
                                           ap3(gsc["g2b"], 0, 2, 1, 2, 0, 8),
                                           OP.mult), ("st", g, "w8b"))
                        vc()
                        vi(v.tensor_add(gat["w8f"][:, (g % 2) * 16:(g % 2) * 16 + 16],
                                        gat["w8a"][:, :], gat["w8b"][:, :]),
                           ("st", g, "w8f"))

            @block.scalar
            def _(s):
                n = [0]
                spend = [False]
                def si(instr, name=None):
                    if spend[0]:
                        instr._wait_ge(ss, n[0])
                        spend[0] = False
                    instr.then_inc(ss, 1)
                    n[0] += 1
                    if name is not None:
                        assert tk.get("S", *name) == n[0], (name, n[0], tk.get("S", *name))
                def sc():
                    spend[0] = True

                def s_ev_ids(i, q0=0, q1=8):
                    g, p = i // 2, i % 2
                    if q0 == 0 and i >= 2:
                        s.wait_ge(st, tk.get("T", "xa", i - 2))
                    for q in range(q0, q1):
                        si(s.activation(sb_xh[p][:, q * 512:(q + 1) * 512],
                                        ps_xt[q % 2][:, :], AF.Identity)
                           ._wait_ge(st, tk.get("T", "tr", i, q)),
                           ("ev", i, q))

                def s_sx2(i):
                    g, il = i // 2, i % 2
                    sc()
                    if i >= 4:
                        s.wait_ge(sv, tk.get("V", "st", g - 2, "v1"))
                        s.wait_ge(sv, tk.get("V", "epi", i - 4, NOC - 1))
                    elif i == 0:
                        s.wait_ge(sv, tk.get("V", "pre", "whl_l"))
                        s.wait_ge(st, tk.get("T", "pre", "wsums"))
                    elif i == 1:
                        s.wait_ge(sv, tk.get("V", "pre", "bl"))
                    col = (g % 2) * 2 + il
                    si(s.activation(sb_tm[i % 2][:, 0:D],
                                    sb_x[g % 3][:, il * D:(il + 1) * D],
                                    AF.Square,
                                    accum_out=sb_s2p[:, col:col + 1]),
                       ("sx2", i, 0))

                for g in range(NG + 1):
                    def s_tmp(j):
                        jl = j % 2
                        gj = j // 2
                        ab = 2 * (gj % 2)
                        s.wait_ge(sv, tk.get("V", "st", gj, "b1"))
                        if j >= 2:
                            s.wait_ge(sv, tk.get("V", "epi", j - 2, NOC - 1))
                        sc()
                        for h in range(2):
                            si(s.activation(
                                sb_tm[j % 2][:, h * (D // 2):(h + 1) * (D // 2)],
                                sb_x[gj % 3][:, jl * D + h * (D // 2):
                                             jl * D + (h + 1) * (D // 2)],
                                AF.Identity,
                                bias=stats["b1"][:, ab + jl:ab + jl + 1],
                                scale=stats["a1"][:, ab + jl:ab + jl + 1]),
                               ("tmp", j, h))
                    if g >= 1:
                        s_tmp(2 * g - 2)
                        s_tmp(2 * g - 1)
                    if g == 0:
                        s_ev_ids(0); s_sx2(0); s_ev_ids(1); s_sx2(1)
                        s_ev_ids(2); s_sx2(2); s_ev_ids(3)
                    if 1 <= g < NG - 1:
                        s_ev_ids(2 * g + 2)
                        s_ev_ids(2 * g + 3)
                        s_sx2(2 * g + 2)
                        s_sx2(2 * g + 3)
                    if g < NG:
                        s.wait_ge(sv, tk.get("V", "st", g, "v1b"))
                        si(s.activation(stats["ln1"][:, :], stats["v1b"][:, :],
                                        AF.Ln, bias=sb_eps[:, :]), ("ln1", g))
                        sc()
                        si(s.activation(stats["inv1"][:, :], stats["ln1"][:, :],
                                        AF.Exp, scale=-0.5), ("inv1", g))
                        s.wait_ge(sv, tk.get("V", "st", g, "v2"))
                        si(s.activation(stats["ln2"][:, :], stats["v2"][:, :],
                                        AF.Ln, bias=sb_eps[:, :]), ("ln2", g))
                        sc()
                        si(s.activation(stats["inv2"][:, :], stats["ln2"][:, :],
                                        AF.Exp, scale=-0.5), ("inv2", g))
                    if g < NG:
                        s.wait_ge(sv, tk.get("V", "st", g, "rneg"))
                        for jl in range(2):
                            si(s.activation(gat["expd"][:, jl * 8:(jl + 1) * 8],
                                            gat["lg2"][:, jl * 8:(jl + 1) * 8],
                                            AF.Exp, bias=gsc["rneg"][:, jl:jl + 1]),
                               ("exp", g, jl))
                    if g == 0:
                        s_sx2(3)

            @block.tensor
            def _(t):
                n = [0]
                def ti(instr, name=None):
                    instr.then_inc(st, 1)
                    n[0] += 1
                    if name is not None:
                        assert tk.get("T", *name) == n[0], (name, n[0], tk.get("T", *name))

                t.wait_ge(sd_w, 5 * 16)
                t.wait_ge(sv, tk.get("V", "pre", "ones128"))
                for c in range(NC):
                    ins = t.matmul(ps_xa[0][:, 0:WC], sb_ones[:, :],
                                   sb_wf[:, c * WC:(c + 1) * WC],
                                   start=(c == 0), stop=(c == NC - 1))
                    if c == NC - 1:
                        ti(ins, ("pre", "wsums"))

                def emit_TR_slab(i, q):
                    g, il = i // 2, i % 2
                    if q == 0:
                        t.wait_ge(sd_in[g], 32)
                    gq = i * 8 + q
                    if gq >= 2:
                        pi, pq = divmod(gq - 2, 8)
                        t.wait_ge(sv, tk.get("V", "lo", pi, pq))
                    for half in range(4):
                        c = q * 4 + half
                        ins = t.transpose(
                            ps_xt[q % 2][:, half * 128:(half + 1) * 128],
                            sb_x[g % 3][:, il * D + c * 128:il * D + (c + 1) * 128],
                            sb_id[:, :])
                        if half == 0 and gq >= 2:
                            ins._wait_ge(ss, tk.get("S", "ev", pi, pq))
                        if half == 3:
                            ti(ins, ("tr", i, q))

                def emit_TR(i):
                    for q in range(8):
                        emit_TR_slab(i, q)

                def emit_XA(i):
                    p = i % 2
                    t.wait_ge(sv, tk.get("V", "lo", i, 7))
                    if i >= 2:
                        t.wait_ge(sv, tk.get("V", "corr2", i - 2))
                    else:
                        t.wait_ge(sv, tk.get("V", "pre", "wsr"))
                    if i >= 4:
                        t.wait_ge(sv, tk.get("V", "hwtl", i - 4))
                    for c in range(NC):
                        t.matmul(ps_xa[p][:, 0:288],
                                 sb_xh[p][:, c * 128:(c + 1) * 128],
                                 sb_whl[:, c * WC2:(c + 1) * WC2],
                                 start=(c == 0), stop=False)
                        ins = t.matmul(ps_xa[p][:, 0:WC],
                                       sb_xl[p][:, c * 128:(c + 1) * 128],
                                       sb_whl[:, c * WC2:c * WC2 + WC],
                                       start=False, stop=(c == NC - 1))
                        if c == NC - 1:
                            ti(ins, ("xa", i))

                def emit_HWT(i):
                    p = i % 2
                    t.wait_ge(sv, tk.get("V", "hw", i))
                    if i >= 2:
                        t.wait_ge(sv, tk.get("V", "hwtl", i - 2))
                    ti(t.transpose(ps_hw[p], sb_hw[p][:, :], sb_id[:, :]),
                       ("hwt", i))

                def emit_Y(i):
                    p = i % 2
                    t.wait_ge(sv, tk.get("V", "hwtl", i))
                    if i == 0:
                        t.wait_ge(sv, tk.get("V", "pre", "bl"))
                    for oc in range(NOC):
                        q = i * NOC + oc
                        ins0 = t.matmul(ps_y[oc % 4][:, :], sb_hwth[p][:, :],
                                        sb_bh[:, oc * OCW:(oc + 1) * OCW],
                                        start=True, stop=False)
                        if q >= 4:
                            pj, poc = divmod(q - 4, NOC)
                            ins0._wait_ge(sv, tk.get("V", "epi", pj, poc))
                        t.matmul(ps_y[oc % 4][:, :], sb_hwth[p][:, :],
                                 sb_bl[:, oc * OCW:(oc + 1) * OCW],
                                 start=False, stop=False)
                        ins = t.matmul(ps_y[oc % 4][:, :], sb_hwtl[p][:, :],
                                       sb_bh[:, oc * OCW:(oc + 1) * OCW],
                                       start=False, stop=True)
                        ti(ins, ("y", i, oc))
                        if i + 4 < NT:
                            emit_TR_slab(i + 4, oc)

                emit_TR(0); emit_TR(1)
                for g in range(NG + 1):
                    if 2 * g < NT:
                        emit_XA(2 * g); emit_XA(2 * g + 1)
                    if g == 0:
                        emit_TR(2); emit_TR(3)
                    if g >= 1:
                        a, b = 2 * g - 2, 2 * g - 1
                        emit_HWT(a); emit_Y(a)
                        emit_HWT(b); emit_Y(b)

    return nc


def kernel(L0_attn_x, L0_ffn_router_w, L0_ffn_router_b, L0_ffn_A, L0_ffn_Bm):
    x = np.ascontiguousarray(np.asarray(L0_attn_x, dtype=np.float32))
    rw = np.asarray(L0_ffn_router_w, dtype=np.float32)
    rb = np.asarray(L0_ffn_router_b, dtype=np.float32)
    A = np.asarray(L0_ffn_A, dtype=np.float32)
    Bm = np.asarray(L0_ffn_Bm, dtype=np.float32)

    a_fm = np.ascontiguousarray(A.reshape(NH, D).T)
    rw_fm = np.ascontiguousarray(rw.T)
    b_fm = np.ascontiguousarray(Bm.transpose(0, 2, 1).reshape(NH, D))
    rb_rep = np.ascontiguousarray(np.broadcast_to(rb, (128, NE)))
    iden = np.eye(128, dtype=np.float32)

    nc = build_nc()
    in_maps = []
    for i in range(8):
        xs_i = np.ascontiguousarray(x[:, i * 512:(i + 1) * 512, :].reshape(TOK, D))
        in_maps.append({"xs": xs_i, "a_fm": a_fm, "rw_fm": rw_fm, "b_fm": b_fm,
                        "rb_rep": rb_rep, "iden": iden})
    res = run_bass_kernel_spmd(nc, in_maps, core_ids=list(range(8)))
    out = np.empty((4, 4096, 4096), dtype=np.float32)
    for i in range(8):
        out[:, i * 512:(i + 1) * 512, :] = np.asarray(res.results[i]["out"]).reshape(4, 512, D)
    return out


# version 29
# speedup vs baseline: 1.0684x; 1.0684x over previous
import numpy as np
from contextlib import ExitStack

import concourse.bass as bass
import concourse.mybir as mybir
from concourse.bass_utils import run_bass_kernel_spmd

F32 = mybir.dt.float32
BF16 = mybir.dt.bfloat16
AX = mybir.AxisListType.X
OP = mybir.AluOpType
AF = mybir.ActivationFunctionType

D = 4096
NC = 32
TOK = 2048
NT = 16
NG = 8
NE = 8
NH = 128
EPS = 1e-5
OCW = 512
NOC = D // OCW
WC = 144
WC2 = 2 * WC
COL_ONES = 136

STATS = ("s2", "m1n", "v1", "msq", "v1b", "ln1", "inv1", "a1", "b1",
         "tt2", "v2", "ln2", "inv2", "alpha", "alph")
GST = ("s2r", "v1", "msq", "v1b", "a1", "b1", "tt2", "v2", "alpha", "alph",
       "lga", "lgb", "rneg",
       "top8a", "top8b", "mask1", "mask2",
       "tsum", "tinv", "g1", "g1b", "g2", "g2b", "w8a", "w8b", "w8f")
VPRE = ("eps", "padz", "ones136", "ones128", "whl_h", "whl_l", "wsr", "bh", "bl")


class Tok:
    def __init__(self):
        self.n = {"T": 0, "S": 0, "V": 0, "G": 0}
        self.named = {}

    def inc(self, eng, name=None):
        self.n[eng] += 1
        if name is not None:
            key = (eng,) + name
            assert key not in self.named, key
            self.named[key] = self.n[eng]
        return self.n[eng]

    def get(self, eng, *name):
        return self.named[(eng,) + name]


def plan_tokens():
    tk = Tok()
    tk.inc("T", ("pre", "wsums"))
    for nm in VPRE:
        tk.inc("V", ("pre", nm))
    def t_TR(i):
        for q in range(8):
            tk.inc("T", ("tr", i, q))
    def t_Y(i):
        for oc in range(NOC):
            tk.inc("T", ("y", i, oc))
            if i + 4 < NT:
                tk.inc("T", ("tr", i + 4, oc))
    t_TR(0); t_TR(1)
    for g in range(NG + 1):
        if 2 * g < NT:
            tk.inc("T", ("xa", 2 * g)); tk.inc("T", ("xa", 2 * g + 1))
        if g == 0:
            t_TR(2); t_TR(3)
        if g >= 1:
            a, b = 2 * g - 2, 2 * g - 1
            tk.inc("T", ("hwt", a)); t_Y(a)
            tk.inc("T", ("hwt", b)); t_Y(b)
    def s_ev_ids(i, q0=0, q1=8):
        for q in range(q0, q1):
            tk.inc("S", ("ev", i, q))
    def s_sx2(i):
        tk.inc("S", ("sx2", i, 0))
    for g in range(NG + 1):
        if g >= 1:
            for j in (2 * g - 2, 2 * g - 1):
                tk.inc("S", ("tmp", j, 0))
                tk.inc("S", ("tmp", j, 1))
        if g == 0:
            s_ev_ids(0); s_sx2(0); s_ev_ids(1); s_sx2(1)
            s_ev_ids(2); s_ev_ids(3); s_sx2(2)
        if 1 <= g < NG - 1:
            s_ev_ids(2 * g + 2)
            s_ev_ids(2 * g + 3)
            s_sx2(2 * g + 2)
        if g < NG:
            tk.inc("S", ("ln1", g))
            tk.inc("S", ("inv1", g))
            tk.inc("S", ("ln2", g))
            tk.inc("S", ("inv2", g))
        if g < NG:
            tk.inc("S", ("exp", g, 0))
            tk.inc("S", ("exp", g, 1))
        if g == 0:
            s_sx2(3)
        if 1 <= g < NG - 1:
            s_sx2(2 * g + 3)
    def v_hw(j):
        for _ in range(NE - 1):
            tk.inc("V")
        tk.inc("V", ("hw", j))
    def v_split(j):
        tk.inc("V", ("hwth", j))
        tk.inc("V", ("hwtl", j))
    def v_tail2(j, o0, o1):
        for oc in range(o0, o1):
            tk.inc("V", ("epi", j, oc))
            if j + 4 < NT:
                tk.inc("V", ("lo", j + 4, oc))
    ST1 = ("v1", "msq", "v1b")
    ST2 = ("a1", "b1", "tt2", "v2")
    ST3 = ("alpha", "alph", "lga", "lgb", "rneg")
    ST4 = ("top8a", "top8b", "mask1", "mask2", "tsum", "tinv",
           "g1", "g1b", "g2", "g2b", "w8a", "w8b", "w8f")
    for i in range(4):
        for q in range(8):
            tk.inc("V", ("lo", i, q))
    def v_corr_p(i):
        tk.inc("V", ("m1n", i))
        tk.inc("V", ("corr", i))
        tk.inc("V", ("corr2", i))
    for g in range(NG + 1):
        if 2 * g < NT:
            v_corr_p(2 * g)
        if g >= 1:
            v_hw(2 * g - 2)
            v_hw(2 * g - 1)
            v_split(2 * g - 2)
        if 2 * g < NT:
            v_corr_p(2 * g + 1)
        if g >= 1:
            v_tail2(2 * g - 2, 0, 7)
            v_split(2 * g - 1)
            v_tail2(2 * g - 2, 7, NOC)
            v_tail2(2 * g - 1, 0, 4 if g == NG - 1 else NOC)
        if g < NG:
            for nm in (ST1 + ST2 + ST3):
                tk.inc("V", ("st", g, nm))
        if g == NG - 1:
            v_tail2(2 * g - 1, 4, NOC)
        if g < NG:
            for nm in ST4:
                tk.inc("V", ("st", g, nm))
    return tk


def build_nc():
    tk = plan_tokens()
    nc = bass.Bass("TRN2", target_bir_lowering=False, debug=False)

    xs = nc.dram_tensor("xs", [TOK, D], F32, kind="ExternalInput")
    a_fm = nc.dram_tensor("a_fm", [D, NH], F32, kind="ExternalInput")
    rw_fm = nc.dram_tensor("rw_fm", [D, NE], F32, kind="ExternalInput")
    b_fm = nc.dram_tensor("b_fm", [NH, D], F32, kind="ExternalInput")
    rb_rep = nc.dram_tensor("rb_rep", [128, NE], F32, kind="ExternalInput")
    iden = nc.dram_tensor("iden", [128, 128], F32, kind="ExternalInput")
    out = nc.dram_tensor("out", [TOK, D], F32, kind="ExternalOutput")

    es = ExitStack()
    def sb(name, shape, dt=F32):
        return es.enter_context(nc.sbuf_tensor(name, shape, dt))
    def ps(name, shape, dt=F32):
        return es.enter_context(nc.psum_tensor(name, shape, dt))
    def sem(name):
        return es.enter_context(nc.semaphore(name))

    sb_x = [sb(f"sb_x{p}", [128, 2 * D]) for p in range(3)]
    sb_xh = [sb(f"sb_xh{p}", [128, D], BF16) for p in range(2)]
    sb_xl = [sb(f"sb_xl{p}", [128, D], BF16) for p in range(2)]
    sb_wf = sb("sb_wf", [128, NC * WC])
    sb_whl = sb("sb_whl", [128, NC * WC2], BF16)
    sb_bf32 = sb("sb_bf32", [128, D])
    sb_bh = sb("sb_bh", [128, D], BF16)
    sb_bl = sb("sb_bl", [128, D], BF16)
    sb_wsr = sb("sb_wsr", [128, WC])
    sb_rb = sb("sb_rb", [128, NE])
    sb_id = sb("sb_id", [128, 128])
    sb_ones = sb("sb_ones", [128, 128])
    sb_eps = sb("sb_eps", [128, 1])
    sb_xap = sb("sb_xap", [128, 4 * WC])
    sb_hw = [sb(f"sb_hw{p}", [128, 128]) for p in range(2)]
    sb_hwth = [sb(f"sb_hwth{p}", [128, 128], BF16) for p in range(2)]
    sb_hwtl = [sb(f"sb_hwtl{p}", [128, 128], BF16) for p in range(2)]
    stats = {nm: sb("st_" + nm, [128, 4 if nm in ('a1', 'b1') else 2]) for nm in STATS}
    sb_s2p = sb("sb_s2p", [128, 16])
    gat = {nm: sb("gt_" + nm, [128, 4 * NE if nm == "w8f" else 2 * NE]) for nm in
           ("lg", "lg2", "expd", "top8", "mask1", "mask2", "w8a", "w8b", "w8f")}
    gsc = {nm: sb("gs_" + nm, [128, 2]) for nm in
           ("rneg", "tsum", "tinv", "g1", "g1b", "g2", "g2b")}
    sb_tm = [sb_wf, sb_bf32]

    ps_xt = [ps(f"ps_xt{p}", [128, 512]) for p in range(2)]
    ps_xa = [ps(f"ps_xa{p}", [128, 512]) for p in range(2)]
    ps_y = [ps(f"ps_y{p}", [128, OCW]) for p in range(4)]
    ps_hw = [ps_xa[0][:, 384:512], ps_xa[1][:, 384:512]]

    sd_w = sem("sd_w")
    sd_in = [sem(f"sd_in{g}") for g in range(NG)]
    sd_out = [sem(f"sd_out{g}") for g in range(NG)]
    st = sem("st"); sv = sem("sv"); ss = sem("ss")

    def ap3(t_, off, pstep, d1, n1, d2, n2):
        return bass.AP(t_, off, [[pstep, 128], [d1, n1], [d2, n2]])
    def g16(t_):
        return ap3(t_, 0, 16, 8, 2, 1, 8)
    def g2x(t_):
        return ap3(t_, 0, 2, 1, 2, 1, 1)

    with es:
        with nc.Block() as block:

            @block.sync
            def _(sync):
                sync.dma_start(
                    bass.AP(sb_wf, 0, [[NC * WC, 128], [WC, NC], [1, NH]]),
                    bass.AP(a_fm, 0, [[NH, 128], [128 * NH, NC], [1, NH]]),
                ).then_inc(sd_w, 16)
                sync.dma_start(
                    bass.AP(sb_wf, 128, [[NC * WC, 128], [WC, NC], [1, NE]]),
                    bass.AP(rw_fm, 0, [[NE, 128], [128 * NE, NC], [1, NE]]),
                ).then_inc(sd_w, 16)
                sync.dma_start(sb_bf32[:, :], b_fm[:, :]).then_inc(sd_w, 16)
                sync.dma_start(sb_rb[:, :], rb_rep[:, :]).then_inc(sd_w, 16)
                sync.dma_start(sb_id[:, :], iden[:, :]).then_inc(sd_w, 16)
                for g in range(3):
                    for il in range(2):
                        sync.dma_start(
                            bass.AP(sb_x[g % 3], il * D, [[2 * D, 128], [1, D]]),
                            bass.AP(xs, (g * 2 + il) * 128 * D, [[D, 128], [1, D]]),
                        ).then_inc(sd_in[g], 16)
                HD = D // 2
                for g in range(NG):
                    for il in range(2):
                        i = g * 2 + il
                        for h in range(2):
                            sync.wait_ge(sv, tk.get("V", "epi", i, 4 * h + 3))
                            sync.dma_start(
                                bass.AP(out, i * 128 * D + h * HD, [[D, 128], [1, HD]]),
                                bass.AP(sb_x[g % 3], il * D + h * HD,
                                        [[2 * D, 128], [1, HD]]),
                            ).then_inc(sd_out[g], 16)
                        if g + 3 < NG:
                            sync.wait_ge(sd_out[g], 32 * (il + 1))
                            sync.dma_start(
                                bass.AP(sb_x[g % 3], il * D, [[2 * D, 128], [1, D]]),
                                bass.AP(xs, ((g + 3) * 2 + il) * 128 * D, [[D, 128], [1, D]]),
                            ).then_inc(sd_in[g + 3], 16)
                for g in range(NG):
                    sync.wait_ge(sd_out[g], 64)

            @block.vector
            def _(v):
                n = [0]
                pend = [False]
                def vi(instr, name=None):
                    if pend[0]:
                        instr._wait_ge(sv, n[0])
                        pend[0] = False
                    instr.then_inc(sv, 1)
                    n[0] += 1
                    if name is not None:
                        assert tk.get("V", *name) == n[0], (name, n[0], tk.get("V", *name))
                def vc():
                    pend[0] = True

                vi(v.memset(sb_eps[:, :], EPS), ("pre", "eps"))
                v.wait_ge(sd_w, 5 * 16)
                vi(v.memset(bass.AP(sb_wf, 137, [[NC * WC, 128], [WC, NC], [1, 7]]), 0.0),
                   ("pre", "padz"))
                vc()
                vi(v.memset(bass.AP(sb_wf, COL_ONES, [[NC * WC, 128], [WC, NC], [1, 1]]), 1.0),
                   ("pre", "ones136"))
                vi(v.memset(sb_ones[:, :], 1.0), ("pre", "ones128"))
                vc()
                vi(v.tensor_copy(
                    bass.AP(sb_whl, 0, [[NC * WC2, 128], [WC2, NC], [1, WC]]),
                    bass.AP(sb_wf, 0, [[NC * WC, 128], [WC, NC], [1, WC]])),
                   ("pre", "whl_h"))
                vc()
                vi(v.tensor_tensor(
                    bass.AP(sb_whl, WC, [[NC * WC2, 128], [WC2, NC], [1, WC]]),
                    bass.AP(sb_wf, 0, [[NC * WC, 128], [WC, NC], [1, WC]]),
                    bass.AP(sb_whl, 0, [[NC * WC2, 128], [WC2, NC], [1, WC]]),
                    OP.subtract), ("pre", "whl_l"))
                v.wait_ge(st, tk.get("T", "pre", "wsums"))
                vi(v.tensor_copy(sb_wsr[:, :], ps_xa[0][:, 0:WC]), ("pre", "wsr"))
                vi(v.tensor_copy(sb_bh[:, :], sb_bf32[:, :]), ("pre", "bh"))
                vc()
                vi(v.tensor_tensor(sb_bl[:, :], sb_bf32[:, :], sb_bh[:, :], OP.subtract),
                   ("pre", "bl"))

                def v_lo(i, q):
                    p = i % 2
                    vi(v.tensor_tensor(sb_xl[p][:, q * 512:(q + 1) * 512],
                                       ps_xt[q % 2][:, :],
                                       sb_xh[p][:, q * 512:(q + 1) * 512],
                                       OP.subtract)
                       ._wait_ge(ss, tk.get("S", "ev", i, q)), ("lo", i, q))

                for i in range(4):
                    for q in range(8):
                        v_lo(i, q)

                def v_corr(i):
                    g, il, sl = i // 2, i % 2, i % 4
                    v.wait_ge(st, tk.get("T", "xa", i))
                    vi(v.tensor_scalar(stats["m1n"][:, il:il + 1],
                                       ps_xa[i % 2][:, COL_ONES:COL_ONES + 1],
                                       -1.0 / D, None, OP.mult), ("m1n", i))
                    vc()
                    vi(v.scalar_tensor_tensor(sb_xap[:, sl * WC:(sl + 1) * WC],
                                              sb_wsr[:, :], stats["m1n"][:, il:il + 1],
                                              ps_xa[i % 2][:, 0:WC], OP.mult, OP.add),
                       ("corr", i))
                    vc()
                    vi(v.tensor_tensor(sb_xap[:, sl * WC:(sl + 1) * WC],
                                       sb_xap[:, sl * WC:(sl + 1) * WC],
                                       ps_xa[i % 2][:, WC:2 * WC], OP.add), ("corr2", i))

                def v_hw(j):
                    gj, jl, sl = j // 2, j % 2, j % 4
                    wsl = (gj % 2) * 16 + jl * 8
                    pj = j % 2
                    vc()
                    if j >= 2:
                        v.wait_ge(st, tk.get("T", "hwt", j - 2))
                    for e in range(NE):
                        ins = v.tensor_scalar(
                            sb_hw[pj][:, e * 16:(e + 1) * 16],
                            sb_xap[:, sl * WC + e * 16:sl * WC + (e + 1) * 16],
                            gat["w8f"][:, wsl + e:wsl + e + 1],
                            None, OP.mult)
                        if e == NE - 1:
                            vi(ins, ("hw", j))
                        elif e == 0:
                            vi(ins)
                        else:
                            ins.then_inc(sv, 1); n[0] += 1

                def v_split(j):
                    pj = j % 2
                    v.wait_ge(st, tk.get("T", "hwt", j))
                    vi(v.tensor_copy(sb_hwth[pj][:, :], ps_hw[pj]), ("hwth", j))
                    vc()
                    vi(v.tensor_tensor(sb_hwtl[pj][:, :], ps_hw[pj],
                                       sb_hwth[pj][:, :], OP.subtract), ("hwtl", j))

                def v_tail2(j, o0, o1):
                    gj, jl = j // 2, j % 2
                    pj = j % 2
                    if o0 == 0:
                        v.wait_ge(ss, tk.get("S", "tmp", j, 0))
                    for oc in range(o0, o1):
                        if oc == 4:
                            v.wait_ge(ss, tk.get("S", "tmp", j, 1))
                        vi(v.tensor_add(
                            sb_x[gj % 3][:, jl * D + oc * OCW:jl * D + (oc + 1) * OCW],
                            sb_tm[pj][:, oc * OCW:(oc + 1) * OCW],
                            ps_y[oc % 4][:, :])
                           ._wait_ge(st, tk.get("T", "y", j, oc)), ("epi", j, oc))
                        if j + 4 < NT:
                            v_lo(j + 4, oc)

                for g in range(NG + 1):
                    if 2 * g < NT:
                        v_corr(2 * g)
                    if g >= 1:
                        v_hw(2 * g - 2)
                        v_hw(2 * g - 1)
                        v_split(2 * g - 2)
                    if 2 * g < NT:
                        v_corr(2 * g + 1)
                    if g >= 1:
                        v_tail2(2 * g - 2, 0, 7)
                        v_split(2 * g - 1)
                        v_tail2(2 * g - 2, 7, NOC)
                        v_tail2(2 * g - 1, 0, 4 if g == NG - 1 else NOC)
                    ab = 2 * (g % 2)
                    i = 2 * g + 1
                    if g < NG:
                        v.wait_ge(ss, tk.get("S", "sx2", i, 0))
                        vi(v.tensor_scalar(stats["v1"][:, :],
                                           sb_s2p[:, (g % 2) * 2:(g % 2) * 2 + 2],
                                           1.0 / D, None, OP.mult), ("st", g, "v1"))
                        vc(); vi(v.tensor_mul(stats["msq"][:, :], stats["m1n"][:, :],
                                              stats["m1n"][:, :]), ("st", g, "msq"))
                        vc(); vi(v.tensor_sub(stats["v1b"][:, :], stats["v1"][:, :],
                                              stats["msq"][:, :]), ("st", g, "v1b"))
                    if g < NG:
                        v.wait_ge(ss, tk.get("S", "inv1", g))
                        vi(v.tensor_scalar(stats["a1"][:, ab:ab + 2], stats["inv1"][:, :],
                                           1.0, None, OP.add), ("st", g, "a1"))
                        vc(); vi(v.tensor_mul(stats["b1"][:, ab:ab + 2], stats["m1n"][:, :],
                                              stats["inv1"][:, :]), ("st", g, "b1"))
                        vc(); vi(v.tensor_mul(stats["tt2"][:, :], stats["a1"][:, ab:ab + 2],
                                              stats["a1"][:, ab:ab + 2]), ("st", g, "tt2"))
                        vc(); vi(v.tensor_mul(stats["v2"][:, :], stats["tt2"][:, :],
                                              stats["v1b"][:, :]), ("st", g, "v2"))
                    if g < NG:
                        v.wait_ge(ss, tk.get("S", "inv2", g))
                        vi(v.tensor_mul(stats["alpha"][:, :], stats["inv2"][:, :],
                                        stats["a1"][:, ab:ab + 2]), ("st", g, "alpha"))
                        vc(); vi(v.tensor_scalar(stats["alph"][:, :], stats["alpha"][:, :],
                                                 0.5, None, OP.mult), ("st", g, "alph"))
                        vc()
                        vi(v.tensor_tensor(g16(gat["lg"]),
                                           ap3(sb_xap, 128, 4 * WC, WC, 2, 1, 8)
                                           if g % 2 == 0 else
                                           ap3(sb_xap, 2 * WC + 128, 4 * WC, WC, 2, 1, 8),
                                           ap3(stats["alpha"], 0, 2, 1, 2, 0, 8),
                                           OP.mult), ("st", g, "lga"))
                        vc()
                        vi(v.tensor_tensor(g16(gat["lg2"]), g16(gat["lg"]),
                                           ap3(sb_rb, 0, 8, 0, 2, 1, 8), OP.add),
                           ("st", g, "lgb"))
                        vc()
                        vi(v.reduce_max(g2x(gsc["rneg"]), g16(gat["lg2"]), AX, negate=True),
                           ("st", g, "rneg"))
                    if g == NG - 1:
                        v_tail2(2 * g - 1, 4, NOC)
                    if g < NG:
                        v.wait_ge(ss, tk.get("S", "exp", g, 1))
                        vi(v.max(gat["top8"][:, 0:8], gat["expd"][:, 0:8]),
                           ("st", g, "top8a"))
                        vc()
                        vi(v.max(gat["top8"][:, 8:16], gat["expd"][:, 8:16]),
                           ("st", g, "top8b"))
                        m1v = bass.AP(gat["top8"], 0, [[16, 128], [8, 2]])
                        m2v = bass.AP(gat["top8"], 1, [[16, 128], [8, 2]])
                        vc()
                        vi(v.tensor_tensor(g16(gat["mask1"]), g16(gat["expd"]),
                                           bass.AP(gat["top8"], 0, [[16, 128], [8, 2], [0, 8]]),
                                           OP.is_equal), ("st", g, "mask1"))
                        vc()
                        vi(v.tensor_tensor(g16(gat["mask2"]), g16(gat["expd"]),
                                           bass.AP(gat["top8"], 1, [[16, 128], [8, 2], [0, 8]]),
                                           OP.is_equal), ("st", g, "mask2"))
                        vc(); vi(v.tensor_tensor(gsc["tsum"][:, :], m1v, m2v, OP.add),
                                 ("st", g, "tsum"))
                        vc(); vi(v.reciprocal(gsc["tinv"][:, :], gsc["tsum"][:, :]),
                                 ("st", g, "tinv"))
                        vc(); vi(v.tensor_tensor(gsc["g1"][:, :], m1v, gsc["tinv"][:, :],
                                                 OP.mult), ("st", g, "g1"))
                        vc(); vi(v.tensor_mul(gsc["g1b"][:, :], gsc["g1"][:, :],
                                              stats["alph"][:, :]), ("st", g, "g1b"))
                        vc(); vi(v.tensor_tensor(gsc["g2"][:, :], m2v, gsc["tinv"][:, :],
                                                 OP.mult), ("st", g, "g2"))
                        vc(); vi(v.tensor_mul(gsc["g2b"][:, :], gsc["g2"][:, :],
                                              stats["alph"][:, :]), ("st", g, "g2b"))
                        vc()
                        vi(v.tensor_tensor(g16(gat["w8a"]), g16(gat["mask1"]),
                                           ap3(gsc["g1b"], 0, 2, 1, 2, 0, 8),
                                           OP.mult), ("st", g, "w8a"))
                        vc()
                        vi(v.tensor_tensor(g16(gat["w8b"]), g16(gat["mask2"]),
                                           ap3(gsc["g2b"], 0, 2, 1, 2, 0, 8),
                                           OP.mult), ("st", g, "w8b"))
                        vc()
                        vi(v.tensor_add(gat["w8f"][:, (g % 2) * 16:(g % 2) * 16 + 16],
                                        gat["w8a"][:, :], gat["w8b"][:, :]),
                           ("st", g, "w8f"))

            @block.scalar
            def _(s):
                n = [0]
                spend = [False]
                def si(instr, name=None):
                    if spend[0]:
                        instr._wait_ge(ss, n[0])
                        spend[0] = False
                    instr.then_inc(ss, 1)
                    n[0] += 1
                    if name is not None:
                        assert tk.get("S", *name) == n[0], (name, n[0], tk.get("S", *name))
                def sc():
                    spend[0] = True

                def s_ev_ids(i, q0=0, q1=8):
                    g, p = i // 2, i % 2
                    if q0 == 0 and i >= 2:
                        s.wait_ge(st, tk.get("T", "xa", i - 2))
                    for q in range(q0, q1):
                        si(s.activation(sb_xh[p][:, q * 512:(q + 1) * 512],
                                        ps_xt[q % 2][:, :], AF.Identity)
                           ._wait_ge(st, tk.get("T", "tr", i, q)),
                           ("ev", i, q))

                def s_sx2(i):
                    g, il = i // 2, i % 2
                    sc()
                    if i >= 4:
                        s.wait_ge(sv, tk.get("V", "st", g - 2, "v1"))
                        s.wait_ge(sv, tk.get("V", "epi", i - 4, NOC - 1))
                    elif i == 0:
                        s.wait_ge(sv, tk.get("V", "pre", "whl_l"))
                        s.wait_ge(st, tk.get("T", "pre", "wsums"))
                    elif i == 1:
                        s.wait_ge(sv, tk.get("V", "pre", "bl"))
                    col = (g % 2) * 2 + il
                    si(s.activation(sb_tm[i % 2][:, 0:D],
                                    sb_x[g % 3][:, il * D:(il + 1) * D],
                                    AF.Square,
                                    accum_out=sb_s2p[:, col:col + 1]),
                       ("sx2", i, 0))

                for g in range(NG + 1):
                    def s_tmp(j):
                        jl = j % 2
                        gj = j // 2
                        ab = 2 * (gj % 2)
                        s.wait_ge(sv, tk.get("V", "st", gj, "b1"))
                        if j >= 2:
                            s.wait_ge(sv, tk.get("V", "epi", j - 2, NOC - 1))
                        sc()
                        for h in range(2):
                            si(s.activation(
                                sb_tm[j % 2][:, h * (D // 2):(h + 1) * (D // 2)],
                                sb_x[gj % 3][:, jl * D + h * (D // 2):
                                             jl * D + (h + 1) * (D // 2)],
                                AF.Identity,
                                bias=stats["b1"][:, ab + jl:ab + jl + 1],
                                scale=stats["a1"][:, ab + jl:ab + jl + 1]),
                               ("tmp", j, h))
                    if g >= 1:
                        s_tmp(2 * g - 2)
                        s_tmp(2 * g - 1)
                    if g == 0:
                        s_ev_ids(0); s_sx2(0); s_ev_ids(1); s_sx2(1)
                        s_ev_ids(2); s_ev_ids(3); s_sx2(2)
                    if 1 <= g < NG - 1:
                        s_ev_ids(2 * g + 2)
                        s_ev_ids(2 * g + 3)
                        s_sx2(2 * g + 2)
                    if g < NG:
                        s.wait_ge(sv, tk.get("V", "st", g, "v1b"))
                        si(s.activation(stats["ln1"][:, :], stats["v1b"][:, :],
                                        AF.Ln, bias=sb_eps[:, :]), ("ln1", g))
                        sc()
                        si(s.activation(stats["inv1"][:, :], stats["ln1"][:, :],
                                        AF.Exp, scale=-0.5), ("inv1", g))
                        s.wait_ge(sv, tk.get("V", "st", g, "v2"))
                        si(s.activation(stats["ln2"][:, :], stats["v2"][:, :],
                                        AF.Ln, bias=sb_eps[:, :]), ("ln2", g))
                        sc()
                        si(s.activation(stats["inv2"][:, :], stats["ln2"][:, :],
                                        AF.Exp, scale=-0.5), ("inv2", g))
                    if g < NG:
                        s.wait_ge(sv, tk.get("V", "st", g, "rneg"))
                        for jl in range(2):
                            si(s.activation(gat["expd"][:, jl * 8:(jl + 1) * 8],
                                            gat["lg2"][:, jl * 8:(jl + 1) * 8],
                                            AF.Exp, bias=gsc["rneg"][:, jl:jl + 1]),
                               ("exp", g, jl))
                    if g == 0:
                        s_sx2(3)
                    if 1 <= g < NG - 1:
                        s_sx2(2 * g + 3)

            @block.tensor
            def _(t):
                n = [0]
                def ti(instr, name=None):
                    instr.then_inc(st, 1)
                    n[0] += 1
                    if name is not None:
                        assert tk.get("T", *name) == n[0], (name, n[0], tk.get("T", *name))

                t.wait_ge(sd_w, 5 * 16)
                t.wait_ge(sv, tk.get("V", "pre", "ones128"))
                for c in range(NC):
                    ins = t.matmul(ps_xa[0][:, 0:WC], sb_ones[:, :],
                                   sb_wf[:, c * WC:(c + 1) * WC],
                                   start=(c == 0), stop=(c == NC - 1))
                    if c == NC - 1:
                        ti(ins, ("pre", "wsums"))

                def emit_TR_slab(i, q):
                    g, il = i // 2, i % 2
                    if q == 0:
                        t.wait_ge(sd_in[g], 32)
                    gq = i * 8 + q
                    if gq >= 2:
                        pi, pq = divmod(gq - 2, 8)
                        t.wait_ge(sv, tk.get("V", "lo", pi, pq))
                    for half in range(4):
                        c = q * 4 + half
                        ins = t.transpose(
                            ps_xt[q % 2][:, half * 128:(half + 1) * 128],
                            sb_x[g % 3][:, il * D + c * 128:il * D + (c + 1) * 128],
                            sb_id[:, :])
                        if half == 0 and gq >= 2:
                            ins._wait_ge(ss, tk.get("S", "ev", pi, pq))
                        if half == 3:
                            ti(ins, ("tr", i, q))

                def emit_TR(i):
                    for q in range(8):
                        emit_TR_slab(i, q)

                def emit_XA(i):
                    p = i % 2
                    t.wait_ge(sv, tk.get("V", "lo", i, 7))
                    if i >= 2:
                        t.wait_ge(sv, tk.get("V", "corr2", i - 2))
                    else:
                        t.wait_ge(sv, tk.get("V", "pre", "wsr"))
                    if i >= 4:
                        t.wait_ge(sv, tk.get("V", "hwtl", i - 4))
                    for c in range(NC):
                        t.matmul(ps_xa[p][:, 0:288],
                                 sb_xh[p][:, c * 128:(c + 1) * 128],
                                 sb_whl[:, c * WC2:(c + 1) * WC2],
                                 start=(c == 0), stop=False)
                        ins = t.matmul(ps_xa[p][:, 0:WC],
                                       sb_xl[p][:, c * 128:(c + 1) * 128],
                                       sb_whl[:, c * WC2:c * WC2 + WC],
                                       start=False, stop=(c == NC - 1))
                        if c == NC - 1:
                            ti(ins, ("xa", i))

                def emit_HWT(i):
                    p = i % 2
                    t.wait_ge(sv, tk.get("V", "hw", i))
                    if i >= 2:
                        t.wait_ge(sv, tk.get("V", "hwtl", i - 2))
                    ti(t.transpose(ps_hw[p], sb_hw[p][:, :], sb_id[:, :]),
                       ("hwt", i))

                def emit_Y(i):
                    p = i % 2
                    t.wait_ge(sv, tk.get("V", "hwtl", i))
                    if i == 0:
                        t.wait_ge(sv, tk.get("V", "pre", "bl"))
                    for oc in range(NOC):
                        q = i * NOC + oc
                        ins0 = t.matmul(ps_y[oc % 4][:, :], sb_hwth[p][:, :],
                                        sb_bh[:, oc * OCW:(oc + 1) * OCW],
                                        start=True, stop=False)
                        if q >= 4:
                            pj, poc = divmod(q - 4, NOC)
                            ins0._wait_ge(sv, tk.get("V", "epi", pj, poc))
                        t.matmul(ps_y[oc % 4][:, :], sb_hwth[p][:, :],
                                 sb_bl[:, oc * OCW:(oc + 1) * OCW],
                                 start=False, stop=False)
                        ins = t.matmul(ps_y[oc % 4][:, :], sb_hwtl[p][:, :],
                                       sb_bh[:, oc * OCW:(oc + 1) * OCW],
                                       start=False, stop=True)
                        ti(ins, ("y", i, oc))
                        if i + 4 < NT:
                            emit_TR_slab(i + 4, oc)

                emit_TR(0); emit_TR(1)
                for g in range(NG + 1):
                    if 2 * g < NT:
                        emit_XA(2 * g); emit_XA(2 * g + 1)
                    if g == 0:
                        emit_TR(2); emit_TR(3)
                    if g >= 1:
                        a, b = 2 * g - 2, 2 * g - 1
                        emit_HWT(a); emit_Y(a)
                        emit_HWT(b); emit_Y(b)

    return nc


def kernel(L0_attn_x, L0_ffn_router_w, L0_ffn_router_b, L0_ffn_A, L0_ffn_Bm):
    x = np.ascontiguousarray(np.asarray(L0_attn_x, dtype=np.float32))
    rw = np.asarray(L0_ffn_router_w, dtype=np.float32)
    rb = np.asarray(L0_ffn_router_b, dtype=np.float32)
    A = np.asarray(L0_ffn_A, dtype=np.float32)
    Bm = np.asarray(L0_ffn_Bm, dtype=np.float32)

    a_fm = np.ascontiguousarray(A.reshape(NH, D).T)
    rw_fm = np.ascontiguousarray(rw.T)
    b_fm = np.ascontiguousarray(Bm.transpose(0, 2, 1).reshape(NH, D))
    rb_rep = np.ascontiguousarray(np.broadcast_to(rb, (128, NE)))
    iden = np.eye(128, dtype=np.float32)

    nc = build_nc()
    in_maps = []
    for i in range(8):
        xs_i = np.ascontiguousarray(x[:, i * 512:(i + 1) * 512, :].reshape(TOK, D))
        in_maps.append({"xs": xs_i, "a_fm": a_fm, "rw_fm": rw_fm, "b_fm": b_fm,
                        "rb_rep": rb_rep, "iden": iden})
    res = run_bass_kernel_spmd(nc, in_maps, core_ids=list(range(8)))
    out = np.empty((4, 4096, 4096), dtype=np.float32)
    for i in range(8):
        out[:, i * 512:(i + 1) * 512, :] = np.asarray(res.results[i]["out"]).reshape(4, 512, D)
    return out
